# version 03
# baseline (speedup 1.0000x reference)
import numpy as np
import ml_dtypes
from contextlib import ExitStack

import concourse.bass as bass
import concourse.mybir as mybir
from concourse.bass_utils import run_bass_kernel_spmd

F32 = mybir.dt.float32
BF16 = mybir.dt.bfloat16
I32 = mybir.dt.int32
AF = mybir.ActivationFunctionType
ALU = mybir.AluOpType
AX = mybir.AxisListType

NCORES = 8
NSEQ = 2
S = 2048
D = 1024
H = 16
DH = 4096
NTOK = NSEQ * S
QP = 127.0
EXP_SHIFT = -4.0

ENG_SEM = {"tensor": "T", "scalar": "A", "vector": "V", "gpsimd": "P", "sync": "Q"}


class Prog:

    def __init__(self, nc, stack):
        self.nc = nc
        self.ops = []
        self.sems = {}
        self.cnt = {}
        self.tok = {}
        self.hw = {}
        self.autoid = 0
        for name in ["Q", "G", "ST", "T", "A", "V", "P", "C"]:
            self.sems[name] = stack.enter_context(nc.semaphore("s" + name))
            self.cnt[name] = 0

    def add(self, eng, fn, waits=(), sem=None, step=None, dma=False):
        if sem is None:
            sem = ENG_SEM[eng]
        if step is None:
            step = 16 if (dma or sem in ("Q", "G", "ST")) else 1
        self.autoid += 1
        tok = self.autoid
        ws = []
        for w in waits:
            if w is None:
                continue
            assert w in self.tok or any(o[3] == w for o in self.ops), f"unknown token {w}"
            ws.append(w)
        self.ops.append((eng, fn, tuple(ws), tok, sem, step))
        return tok

    def flush(self):
        ops = self.ops
        self.ops = []
        if not ops:
            return
        for (eng, fn, waits, tok, sem, step) in ops:
            self.cnt[sem] += step
            self.tok[tok] = (sem, self.cnt[sem])
        with self.nc.Block() as blk:
            for engname in ("sync", "tensor", "scalar", "vector", "gpsimd"):
                def emit(engobj, engname=engname, ops=ops):
                    for (eng, fn, waits, tok, sem, step) in ops:
                        if eng != engname:
                            continue
                        for w in waits:
                            s_, v_ = self.tok[w]
                            if self.hw.get((engname, s_), 0) < v_:
                                engobj.wait_ge(self.sems[s_], v_)
                                self.hw[(engname, s_)] = v_
                        instr = fn(engobj)
                        instr.then_inc(self.sems[sem], step)
                getattr(blk, engname)(emit)

    def final_wait(self, eng, sem):
        with self.nc.Block() as blk:
            def emit(engobj):
                engobj.wait_ge(self.sems[sem], self.cnt[sem])
            getattr(blk, eng)(emit)


class Buf:
    def __init__(self):
        self.wtok = None
        self.rtoks = []

    def rdeps(self):
        return [self.wtok] if self.wtok is not None else []

    def wdeps(self):
        d = list(self.rtoks)
        if self.wtok is not None:
            d.append(self.wtok)
        return d

    def setw(self, tok):
        self.wtok = tok
        self.rtoks = []

    def addr(self, tok):
        self.rtoks.append(tok)


def build_nc(taps=()):
    nc = bass.Bass("TRN2", target_bir_lowering=False)

    x_tm = nc.dram_tensor("x_tm", [NSEQ, S, D], F32, kind="ExternalInput").ap()
    x_fm = nc.dram_tensor("x_fm", [NSEQ, D, S], F32, kind="ExternalInput").ap()
    wqT = nc.dram_tensor("wqT", [D, D], F32, kind="ExternalInput").ap()
    wkT = nc.dram_tensor("wkT", [D, D], F32, kind="ExternalInput").ap()
    wvT = nc.dram_tensor("wvT", [D, D], F32, kind="ExternalInput").ap()
    woT = nc.dram_tensor("woT", [D, D], F32, kind="ExternalInput").ap()
    w1T = nc.dram_tensor("w1T", [D, DH], F32, kind="ExternalInput").ap()
    w2T = nc.dram_tensor("w2T", [DH, D], F32, kind="ExternalInput").ap()
    b1c = nc.dram_tensor("b1c", [128, 32], F32, kind="ExternalInput").ap()
    b2r = nc.dram_tensor("b2r", [1, D], F32, kind="ExternalInput").ap()
    out = nc.dram_tensor("out", [NSEQ, S, D], F32, kind="ExternalOutput").ap()

    x1sp = nc.dram_tensor("x1sp", [NTOK, D], F32, kind="Internal").ap()
    hsp = nc.dram_tensor("hsp", [8, 32, 128, 512], F32, kind="Internal").ap()
    cci1 = nc.dram_tensor("cci1", [1, 8], F32, kind="Internal").ap()
    cco1 = nc.dram_tensor("cco1", [1, 8], F32, kind="Internal", addr_space="Shared").ap()
    cci2 = nc.dram_tensor("cci2", [1, 8], F32, kind="Internal").ap()
    cco2 = nc.dram_tensor("cco2", [1, 8], F32, kind="Internal", addr_space="Shared").ap()

    tap_aps = {}
    if "x1" in taps:
        tap_aps["tap_x1"] = nc.dram_tensor("tap_x1", [NTOK, D], F32, kind="ExternalOutput").ap()
    if "h" in taps:
        tap_aps["tap_h"] = nc.dram_tensor("tap_h", [8, 32, 128, 512], F32, kind="ExternalOutput").ap()
    if "qkv" in taps:
        tap_aps["tap_qkv"] = nc.dram_tensor("tap_qkv", [128, 49408], BF16, kind="ExternalOutput").ap()
    if "ao" in taps:
        tap_aps["tap_ao"] = nc.dram_tensor("tap_ao", [128, 16384], BF16, kind="ExternalOutput").ap()
    if "ffn" in taps:
        tap_aps["tap_scl"] = nc.dram_tensor("tap_scl", [128, 8], F32, kind="ExternalOutput").ap()
        tap_aps["tap_xq"] = nc.dram_tensor("tap_xq", [128, 8192], BF16, kind="ExternalOutput").ap()
        tap_aps["tap_w1q"] = nc.dram_tensor("tap_w1q", [128, 32768], BF16, kind="ExternalOutput").ap()

    with ExitStack() as stack:
        e = stack.enter_context
        PB = e(nc.sbuf_tensor("PB", [128, 84480], BF16))
        PF = e(nc.sbuf_tensor("PF", [128, 5120], F32))
        WST = e(nc.sbuf_tensor("WST", [128, 1024], F32))
        XQI = e(nc.sbuf_tensor("XQI", [128, 1024], I32))
        HQI = e(nc.sbuf_tensor("HQI", [128, 1024], I32))
        B1S = e(nc.sbuf_tensor("B1S", [128, 32], F32))
        B2R = e(nc.sbuf_tensor("B2R", [1, D], F32))
        SCL = e(nc.sbuf_tensor("SCL", [128, 8], F32))
        SC1 = e(nc.sbuf_tensor("SC1", [1, 16], F32))
        ONE = e(nc.sbuf_tensor("ONE", [1, 128], F32))
        ON3 = e(nc.sbuf_tensor("ON3", [33, 128], F32))
        IFP = e(nc.sbuf_tensor("IFP", [128, 128], F32))
        MX = e(nc.sbuf_tensor("MX", [128, 4], F32))
        ST8 = e(nc.sbuf_tensor("ST8", [128, 48], F32))
        PTR = e(nc.psum_tensor("PTR", [1, 128], F32))
        PSM = e(nc.psum_tensor("PSM", [128, 512], F32))

        R_XA = 0; R_Q = 16384; R_K = 32768; R_V = 49152
        R_W0 = 65792; R_W1 = 73984; R_EX = 82176; R_U = 84224; R_I = 84352; R_XQT = 82176
        R_W1Q = 0; R_W2Q = 32768; R_HQ = 0

        F_XST = 0
        F_RES = 2048
        F_HST = 4096
        F_HSL = 0
        F_X1B = 1024
        F_OST = 3072

        def pb(off, width, part=(0, 128)):
            return bass.AP(PB, part[0] * 84480 + off, [[84480, part[1] - part[0]], [1, width]])

        def pb3(off, pattern, part=(0, 128)):
            return bass.AP(PB, part[0] * 84480 + off, [[84480, part[1] - part[0]]] + pattern)

        def pf(off, width):
            return bass.AP(PF, off, [[5120, 128], [1, width]])

        prog = Prog(nc, stack)
        add = prog.add

        b_xbf = Buf(); b_ao = Buf()
        b_Q = Buf(); b_K = Buf(); b_V = Buf()
        b_w = [Buf(), Buf()]
        b_ex = [Buf(), Buf()]
        b_P0R = [Buf(), Buf()]; b_NB = Buf()
        b_wst = [Buf(), Buf()]
        b_pf = {}
        b_mx = [Buf() for _ in range(4)]
        b_scl = Buf(); b_sc1 = Buf(); b_b1s = Buf(); b_b2r = Buf()
        b_st8 = Buf()
        b_st8p = [Buf(), Buf()]
        b_stw = Buf(); b_sth = Buf(); b_stm = Buf()
        b_xqi = Buf(); b_hqi = [Buf(), Buf()]
        b_ptrg = Buf(); b_psm = Buf()

        def pfbuf(key):
            if key not in b_pf:
                b_pf[key] = Buf()
            return b_pf[key]

        spill_x1 = {}
        spill_h = {}

        def ln_chain(par, src_off, dst_off, in_waits, dst_waits):
            sb = par * 16
            st8b = b_st8p[par]
            s1t = add("vector", lambda v, sb=sb: v.bn_stats(
                out=ST8[:, sb + 0:sb + 6], in_=pf(src_off, 512)),
                waits=list(in_waits) + st8b.wdeps())
            s2t = add("vector", lambda v, sb=sb: v.bn_stats(
                out=ST8[:, sb + 6:sb + 12], in_=pf(src_off + 512, 512)),
                waits=list(in_waits) + [s1t])
            agg = add("vector", lambda v, sb=sb: v.bn_aggr(
                out=ST8[:, sb + 12:sb + 14], in_=ST8[:, sb + 0:sb + 12]), waits=[s2t])
            vpe = add("vector", lambda v, sb=sb: v.tensor_scalar(
                out=ST8[:, sb + 14:sb + 15], in0=ST8[:, sb + 13:sb + 14],
                scalar1=1e-5, scalar2=None, op0=ALU.add), waits=[agg])
            rec = add("vector", lambda v, sb=sb: v.reciprocal(
                out=ST8[:, sb + 15:sb + 16], in_=ST8[:, sb + 14:sb + 15]), waits=[vpe])
            rst = add("scalar", lambda sce, sb=sb: sce.activation(
                out=ST8[:, sb + 14:sb + 15], in_=ST8[:, sb + 15:sb + 16], func=AF.Sqrt),
                waits=[rec])
            ngm = add("vector", lambda v, sb=sb: v.scalar_tensor_tensor(
                out=ST8[:, sb + 15:sb + 16], in0=ST8[:, sb + 12:sb + 13], scalar=-1.0,
                in1=ST8[:, sb + 14:sb + 15], op0=ALU.mult, op1=ALU.mult), waits=[rst])
            st8b.setw(ngm)
            ap = add("scalar", lambda sce, sb=sb: sce.activation(
                out=pf(dst_off, 1024), in_=pf(src_off, 1024), func=AF.Identity,
                bias=ST8[:, sb + 15:sb + 16], scale=ST8[:, sb + 14:sb + 15]),
                waits=[ngm] + list(dst_waits))
            st8b.addr(ap)
            return ap

        with ExitStack() as ps:
            c_one = add("gpsimd", lambda g: g.memset(ONE[:], 1.0))
            c_on3 = add("gpsimd", lambda g: g.memset(ON3[0:1, :], 1.0))
            c_on3b = add("gpsimd", lambda g: g.memset(ON3[32:33, :], 1.0))
            c_iota = add("gpsimd", lambda g: g.iota(out=XQI[:, 0:128], pattern=[[1, 128]], base=0,
                                                    channel_multiplier=-1,
                                                    allow_small_or_imprecise_dtypes=True))
            c_u = add("vector", lambda v: v.tensor_scalar(out=pb(R_U, 128), in0=XQI[:, 0:128],
                                                          scalar1=0.0, scalar2=None, op0=ALU.is_ge),
                      waits=[c_iota])
            c_idn = add("vector", lambda v: v.tensor_scalar(out=pb(R_I, 128), in0=XQI[:, 0:128],
                                                            scalar1=0.0, scalar2=None, op0=ALU.is_equal),
                        waits=[c_iota])
            c_if = add("vector", lambda v: v.tensor_scalar(out=IFP[:], in0=XQI[:, 0:128],
                                                           scalar1=0.0, scalar2=None, op0=ALU.is_equal),
                       waits=[c_iota])
            b_xqi.setw(c_iota)
            b_xqi.addr(c_u); b_xqi.addr(c_idn); b_xqi.addr(c_if)
            c_vones = add("gpsimd", lambda g: g.memset(
                bass.AP(PB, R_V + 64, [[84480, 128], [1040, 16], [65, 16], [1, 1]]), 1.0))
            b_V.setw(c_vones)

            t = add("sync", lambda sy: sy.dma_start(out=B1S[:], in_=b1c), dma=True)
            b_b1s.setw(t)
            t = add("sync", lambda sy: sy.dma_start(out=B2R[:], in_=b2r), dma=True)
            b_b2r.setw(t)
            t = add("gpsimd", lambda g: g.memset(MX[:], 0.0))
            for b in b_mx:
                b.setw(t)
            c_eb = add("gpsimd", lambda g: g.memset(SCL[:, 6:7], EXP_SHIFT))
            b_scl.setw(c_eb)

            w1v = w1T.rearrange("(c p) h -> p c h", p=128)
            w2v = w2T.rearrange("(c p) d -> p c d", p=128)
            for wv, nsub, mxi in [(w1v, 8, 0), (w2v, 2, 1)]:
                nch = 64
                for ci in range(nch):
                    sl = ci % 2
                    bslot = b_wst[sl]
                    cc_, off_ = ci // nsub, (ci % nsub) * 512
                    t = add("sync", lambda sy, wv=wv, cc_=cc_, off_=off_, sl=sl: sy.dma_start(
                        out=WST[:, sl * 512:(sl + 1) * 512], in_=wv[:, cc_, off_:off_ + 512]),
                        waits=bslot.wdeps(), dma=True)
                    bslot.setw(t)
                    t2 = add("vector", lambda v, sl=sl: v.tensor_reduce(
                        out=ST8[:, 32:33], in_=WST[:, sl * 512:(sl + 1) * 512],
                        axis=AX.X, op=ALU.max, apply_absolute_value=True),
                        waits=[t] + b_stw.wdeps())
                    bslot.addr(t2)
                    b_stw.setw(t2)
                    t3 = add("vector", lambda v, mxi=mxi: v.tensor_tensor(
                        out=MX[:, mxi:mxi + 1], in0=MX[:, mxi:mxi + 1], in1=ST8[:, 32:33], op=ALU.max),
                        waits=[t2] + b_mx[mxi].wdeps())
                    b_stw.addr(t3)
                    b_mx[mxi].setw(t3)
            for mxi in range(2):
                tt = add("tensor", lambda te, mxi=mxi: te.transpose(
                    out=PTR[:], in_=MX[:, mxi:mxi + 1], identity=IFP[:]),
                    waits=b_mx[mxi].rdeps() + [c_if] + b_ptrg.wdeps())
                b_mx[mxi].addr(tt)
                b_ptrg.setw(tt)
                tr = add("vector", lambda v, mxi=mxi: v.tensor_reduce(
                    out=SC1[:, mxi:mxi + 1], in_=PTR[0:1, :], axis=AX.X, op=ALU.max),
                    waits=[tt] + b_sc1.wdeps())
                b_ptrg.addr(tr)
                b_sc1.setw(tr)
            prog.flush()

        wload = {}
        for s in range(NSEQ):
            with ExitStack() as ps:
                pj = [ps.enter_context(nc.psum_tensor(f"pj{s}_{i}", [128, 512], F32)) for i in range(4)]
                b_pj = [Buf() for _ in range(4)]

                xld = []
                for dc in range(8):
                    t = add("gpsimd", lambda g, s=s, dc=dc: g.dma_start(
                        out=pb(R_XA + dc * 2048, 2048), in_=x_fm[s, dc * 128:(dc + 1) * 128, :]),
                        waits=(b_xbf.wdeps() + b_ao.wdeps() if dc == 0 else []), sem="G", dma=True)
                    xld.append(t)
                b_xbf.setw(xld[-1])

                def load_w(wi, wap, slot, s=s):
                    wvv = wap.rearrange("(c p) f -> p c f", p=128)
                    t = add("gpsimd", lambda g, wvv=wvv, slot=slot: g.dma_start(
                        out=pb3((R_W0, R_W1)[slot], [[1024, 8], [1, 1024]]), in_=wvv),
                        waits=b_w[slot].wdeps(), sem="G", dma=True)
                    b_w[slot].setw(t)
                    wload[(s, wi)] = (t, slot)

                load_w(0, wqT, 0)
                load_w(1, wkT, 1)

                for wi, (breg, roff) in enumerate([(b_Q, R_Q), (b_K, R_K)]):
                    if wi == 1:
                        load_w(2, wvT, 0)
                    wtok, slot = wload[(s, wi)]
                    woffb = (R_W0, R_W1)[slot]
                    first = True
                    for oc in range(8):
                        for dc in range(8):
                            for ts4 in range(4):
                                pjb = b_pj[ts4]
                                deps = [wtok, xld[dc]]
                                if dc == 0:
                                    deps = deps + pjb.wdeps()
                                mtok = add("tensor", lambda te, oc=oc, dc=dc, ts4=ts4, woffb=woffb,
                                           pp=pj[ts4]: te.matmul(
                                    out=pp[:],
                                    lhsT=pb(woffb + dc * 1024 + oc * 128, 128),
                                    rhs=pb(R_XA + dc * 2048 + ts4 * 512, 512),
                                    start=(dc == 0), stop=(dc == 7)),
                                    waits=deps)
                                pjb.setw(mtok)
                        b_w[slot].addr(mtok)
                        b_xbf.addr(mtok)
                        for ts4 in range(4):
                            pjb = b_pj[ts4]
                            extra = breg.wdeps() if first else []
                            first = False
                            ev = add("vector", lambda v, oc=oc, ts4=ts4, roff=roff,
                                     pp=pj[ts4]: v.tensor_copy(
                                out=pb(roff + oc * 2048 + ts4 * 512, 512), in_=pp[:]),
                                waits=[pjb.wtok] + extra)
                            pjb.addr(ev)
                    breg.setw(ev)

                load_w(3, woT, 1)
                wtok, slot = wload[(s, 2)]
                woffb = (R_W0, R_W1)[slot]
                first = True
                for tt in range(16):
                    bank0 = (tt % 2) * 2
                    for dc in range(8):
                        for vh in range(2):
                            pjb = b_pj[bank0 + vh]
                            deps = [wtok, xld[dc]]
                            if dc == 0:
                                deps = deps + pjb.wdeps()
                            mtok = add("tensor", lambda te, tt=tt, vh=vh, dc=dc, woffb=woffb,
                                       pp=pj[bank0 + vh]: te.matmul(
                                out=pp[:],
                                lhsT=pb(R_XA + dc * 2048 + tt * 128, 128),
                                rhs=pb(woffb + dc * 1024 + vh * 512, 512),
                                start=(dc == 0), stop=(dc == 7)),
                                waits=deps)
                            pjb.setw(mtok)
                    b_w[slot].addr(mtok)
                    b_xbf.addr(mtok)
                    for vh in range(2):
                        pjb = b_pj[bank0 + vh]
                        extra = b_V.wdeps() if first else []
                        first = False
                        ev = add("vector", lambda v, tt=tt, vh=vh, pp=pj[bank0 + vh]: v.tensor_copy(
                            out=pb3(R_V + tt * 1040 + vh * 8 * 65, [[65, 8], [1, 64]]), in_=pp[:]),
                            waits=[pjb.wtok] + extra)
                        pjb.addr(ev)
                b_V.setw(ev)
                if "qkv" in taps and s == 0:
                    tq = add("sync", lambda sy: sy.dma_start(
                        out=tap_aps["tap_qkv"], in_=pb(R_Q, 49408)),
                        waits=b_Q.rdeps() + b_K.rdeps() + b_V.rdeps(), sem="ST", dma=True)
                    b_Q.addr(tq); b_K.addr(tq); b_V.addr(tq)
                prog.flush()

            with ExitStack() as ps:
                psc = [ps.enter_context(nc.psum_tensor(f"psc{s}_{i}", [128, 1024], F32)) for i in range(2)]
                pao = [ps.enter_context(nc.psum_tensor(f"pao{s}_{i}", [65, 512], F32)) for i in range(2)]
                b_psc = [Buf(), Buf()]
                b_pao = [Buf(), Buf()]

                def emit_av(hp, qs, kc, nkc, offs, width, exoff, avdep, scbuf):
                    for hh in range(2):
                        pab = b_pao[hh]
                        deps = list(avdep) + b_V.rdeps()
                        if kc == 0:
                            deps = deps + pab.wdeps()
                        av = add("tensor", lambda te, hp=hp, hh=hh, kc=kc, offs=offs, width=width,
                                 exoff=exoff, pp=pao[hh]: te.matmul(
                            out=bass.AP(pp, offs, [[512, 65], [1, width]]),
                            lhsT=pb(R_V + kc * 1040 + (2 * hp + hh) * 65, 65),
                            rhs=pb(exoff + hh * 512, width),
                            start=(kc == 0), stop=(kc == nkc - 1)),
                            waits=deps)
                        pab.setw(av)
                        b_ex[scbuf].addr(av)
                        b_V.addr(av)

                deferred_fin = [None]

                def run_deferred():
                    if deferred_fin[0] is not None:
                        deferred_fin[0]()
                        deferred_fin[0] = None

                for hp in range(8):
                    for qs in range(4):
                        nkc = 4 * qs + 4
                        pending = None
                        for kc in range(nkc):
                            offs = max(0, 128 * kc - 512 * qs)
                            width = 512 - offs
                            qstart = 512 * qs + offs
                            pb_sc = b_psc[kc % 2]
                            pb_ex = b_ex[kc % 2]
                            sc = psc[kc % 2]
                            deps = b_Q.rdeps() + b_K.rdeps() + pb_sc.wdeps()
                            mA = add("tensor", lambda te, hp=hp, kc=kc, qstart=qstart, width=width,
                                     sc=sc: te.matmul(
                                out=bass.AP(sc, 0, [[1024, 128], [1, width]]),
                                lhsT=pb(R_K + hp * 2048 + kc * 128, 128, part=(0, 64)),
                                rhs=pb(R_Q + hp * 2048 + qstart, width, part=(0, 64)),
                                start=True, stop=True, tile_position=(0, 0)),
                                waits=deps)
                            mB = add("tensor", lambda te, hp=hp, kc=kc, qstart=qstart, width=width,
                                     sc=sc: te.matmul(
                                out=bass.AP(sc, 512, [[1024, 128], [1, width]]),
                                lhsT=pb(R_K + hp * 2048 + kc * 128, 128, part=(64, 128)),
                                rhs=pb(R_Q + hp * 2048 + qstart, width, part=(64, 128)),
                                start=True, stop=True, tile_position=(64, 0)),
                                waits=deps)
                            pb_sc.setw(mB)
                            b_Q.addr(mB); b_K.addr(mB)
                            exoff = R_EX + (kc % 2) * 1024
                            ex = add("scalar", lambda sce, sc=sc, width=width, exoff=exoff: sce.activation(
                                out=bass.AP(PB, exoff, [[84480, 128], [512, 2], [1, width]]),
                                in_=bass.AP(sc, 0, [[1024, 128], [512, 2], [1, width]]),
                                func=AF.Exp, bias=SCL[:, 6:7], scale=0.125),
                                waits=[mA, mB, c_eb] + pb_ex.wdeps())
                            pb_sc.addr(ex)
                            pb_ex.setw(ex)
                            avdep = [ex]
                            if kc >= 4 * qs:
                                m1 = add("vector", lambda v, exoff=exoff: v.tensor_tensor(
                                    out=pb(exoff, 128), in0=pb(exoff, 128), in1=pb(R_U, 128), op=ALU.mult),
                                    waits=[ex, c_u])
                                m2 = add("vector", lambda v, exoff=exoff: v.tensor_tensor(
                                    out=pb(exoff + 512, 128), in0=pb(exoff + 512, 128), in1=pb(R_U, 128),
                                    op=ALU.mult),
                                    waits=[ex, c_u])
                                pb_ex.setw(m2)
                                avdep = [m1, m2]
                            if pending is not None:
                                emit_av(*pending)
                            if kc == 1:
                                run_deferred()
                            pending = (hp, qs, kc, nkc, offs, width, exoff, avdep, kc % 2)
                        emit_av(*pending)
                        ats = []
                        zts = []
                        for hh in range(2):
                            pab = b_pao[hh]
                            at = add("vector", lambda v, hp=hp, hh=hh, qs=qs, pp=pao[hh]: v.tensor_copy(
                                out=pb(R_XA + hp * 2048 + qs * 512, 512, part=(64 * hh, 64 * hh + 64)),
                                in_=pp[0:64, :]),
                                waits=pab.rdeps() + b_ao.wdeps())
                            b_ao.setw(at)
                            ats.append(at)
                            zt = add("vector", lambda v, hh=hh, pp=pao[hh]: v.tensor_copy(
                                out=bass.AP(PF, hh * 32 * 5120 + 4608, [[5120, 1], [1, 512]]),
                                in_=pp[64:65, :]),
                                waits=pab.rdeps() + b_P0R[hh].wdeps())
                            b_P0R[hh].setw(zt)
                            zts.append(zt)
                            pab.addr(zt); pab.addr(at)

                        def fin(hp=hp, qs=qs, ats=ats, zts=zts):
                            bms = []
                            for hh in range(2):
                                bm = add("tensor", lambda te, hh=hh: te.matmul(
                                    out=PSM[64 * hh:64 * hh + 64, :],
                                    lhsT=ON3[32 * hh:32 * hh + 1, 0:64],
                                    rhs=bass.AP(PF, hh * 32 * 5120 + 4608, [[5120, 1], [1, 512]]),
                                    start=True, stop=True),
                                    waits=[zts[hh], c_on3, c_on3b] + (b_psm.wdeps() if hh == 0 else []))
                                bms.append(bm)
                                b_P0R[hh].addr(bm)
                            b_psm.setw(bms[-1])
                            nb = add("vector", lambda v: v.reciprocal(
                                out=pf(4096, 512), in_=PSM[:]),
                                waits=bms + b_NB.wdeps())
                            b_psm.addr(nb)
                            b_NB.setw(nb)
                            nm = add("vector", lambda v, hp=hp, qs=qs: v.tensor_tensor(
                                out=pb(R_XA + hp * 2048 + qs * 512, 512),
                                in0=pb(R_XA + hp * 2048 + qs * 512, 512),
                                in1=pf(4096, 512), op=ALU.mult),
                                waits=[nb] + ats)
                            b_NB.addr(nm)
                            b_ao.setw(nm)
                        deferred_fin[0] = fin
                run_deferred()
                if "ao" in taps and s == 0:
                    ta = add("sync", lambda sy: sy.dma_start(
                        out=tap_aps["tap_ao"], in_=pb(R_XA, 16384)),
                        waits=b_ao.rdeps(), sem="ST", dma=True)
                    b_ao.addr(ta)
                prog.flush()

            with ExitStack() as ps:
                po = [ps.enter_context(nc.psum_tensor(f"po{s}_{i}", [128, 512], F32)) for i in range(4)]
                b_po = [Buf() for _ in range(4)]
                wtok, slot = wload[(s, 3)]
                woffb = (R_W0, R_W1)[slot]
                last_mm = None
                for tt in range(16):
                    bank0 = (tt % 2) * 2
                    bx = pfbuf(("xst", tt % 2))
                    xr = add("sync", lambda sy, s=s, tt=tt, sl=tt % 2: sy.dma_start(
                        out=pf(F_XST + sl * 1024, 1024), in_=x_tm[s, tt * 128:(tt + 1) * 128, :]),
                        waits=bx.wdeps(), dma=True)
                    bx.setw(xr)
                    for vc in range(8):
                        for dh in range(2):
                            pjb = b_po[bank0 + dh]
                            deps = [wtok, b_ao.wtok]
                            if vc == 0:
                                deps = deps + pjb.wdeps()
                            mtok = add("tensor", lambda te, tt=tt, dh=dh, vc=vc, woffb=woffb,
                                       pp=po[bank0 + dh]: te.matmul(
                                out=pp[:],
                                lhsT=pb(R_XA + vc * 2048 + tt * 128, 128),
                                rhs=pb(woffb + vc * 1024 + dh * 512, 512),
                                start=(vc == 0), stop=(vc == 7)),
                                waits=deps)
                            pjb.setw(mtok)
                            last_mm = mtok
                    b_ao.addr(last_mm)
                    br = pfbuf(("res", tt % 2))
                    for dh in range(2):
                        pjb = b_po[bank0 + dh]
                        rv = add("vector", lambda v, dh=dh, sl=tt % 2, pp=po[bank0 + dh]: v.tensor_tensor(
                            out=pf(F_RES + sl * 1024 + dh * 512, 512),
                            in0=pp[:],
                            in1=pf(F_XST + sl * 1024 + dh * 512, 512), op=ALU.add),
                            waits=[pjb.wtok, xr] + (br.wdeps() if dh == 0 else []))
                        pjb.addr(rv)
                        bx.addr(rv)
                        if dh == 0:
                            br.setw(rv)
                        else:
                            br.wtok = rv
                    x1t = ln_chain(tt % 2, F_RES + (tt % 2) * 1024, F_XST + (tt % 2) * 1024,
                                   [br.wtok], bx.wdeps())
                    bx.setw(x1t)
                    br.addr(x1t)
                    mx1 = add("vector", lambda v, sl=tt % 2: v.tensor_reduce(
                        out=ST8[:, 34:35], in_=pf(F_XST + sl * 1024, 1024), axis=AX.X, op=ALU.max,
                        apply_absolute_value=True),
                        waits=[x1t] + b_stm.wdeps())
                    b_stm.setw(mx1)
                    mx2 = add("vector", lambda v: v.tensor_tensor(
                        out=MX[:, 2:3], in0=MX[:, 2:3], in1=ST8[:, 34:35], op=ALU.max),
                        waits=[mx1] + b_mx[2].wdeps())
                    b_mx[2].setw(mx2)
                    b_stm.addr(mx2)
                    bx.addr(mx1)
                    g = s * 16 + tt
                    sp = add("sync", lambda sy, g=g, sl=tt % 2: sy.dma_start(
                        out=x1sp[g * 128:(g + 1) * 128, :], in_=pf(F_XST + sl * 1024, 1024)),
                        waits=[x1t], sem="ST", dma=True)
                    bx.addr(sp)
                    spill_x1[g] = sp
                    if "x1" in taps:
                        sp2 = add("sync", lambda sy, g=g, sl=tt % 2: sy.dma_start(
                            out=tap_aps["tap_x1"][g * 128:(g + 1) * 128, :], in_=pf(F_XST + sl * 1024, 1024)),
                            waits=[x1t], sem="ST", dma=True)
                        bx.addr(sp2)
                b_w[slot].addr(last_mm)
                prog.flush()

        with ExitStack() as ps:
            tt4 = add("tensor", lambda te: te.transpose(out=PTR[:], in_=MX[:, 2:3], identity=IFP[:]),
                      waits=b_mx[2].rdeps() + [c_if] + b_ptrg.wdeps())
            b_mx[2].addr(tt4)
            b_ptrg.setw(tt4)
            t2 = add("vector", lambda v: v.tensor_reduce(
                out=SC1[:, 2:3], in_=PTR[0:1, :], axis=AX.X, op=ALU.max),
                waits=[tt4] + b_sc1.wdeps())
            b_ptrg.addr(t2)
            b_sc1.setw(t2)
            zz = add("gpsimd", lambda g: g.memset(SC1[:, 8:16], 0.0), waits=[t2])
            cp = add("vector", lambda v: v.tensor_copy(out=SC1[:, 8:9], in_=SC1[:, 2:3]), waits=[zz])
            b_sc1.setw(cp)
            d1 = add("sync", lambda sy: sy.dma_start(out=cci1, in_=SC1[:, 8:16]), waits=[cp],
                     sem="ST", dma=True)
            cc1 = add("gpsimd", lambda g: g.collective_compute(
                "AllReduce", ALU.max, replica_groups=[list(range(NCORES))],
                ins=[cci1], outs=[cco1]), waits=[d1], sem="C", step=1)
            d2 = add("sync", lambda sy: sy.dma_start(out=SC1[:, 8:16], in_=cco1), waits=[cc1], dma=True)
            b_sc1.setw(d2)
            c1 = add("vector", lambda v: v.tensor_scalar(
                out=SC1[:, 3:4], in0=SC1[:, 8:9], scalar1=1e-6, scalar2=1.0 / QP,
                op0=ALU.max, op1=ALU.mult), waits=[d2])
            c2 = add("vector", lambda v: v.tensor_scalar(
                out=SC1[:, 4:5], in0=SC1[:, 0:1], scalar1=1e-6, scalar2=1.0 / QP,
                op0=ALU.max, op1=ALU.mult), waits=[d2])
            c3 = add("vector", lambda v: v.reciprocal(out=SC1[:, 5:6], in_=SC1[:, 3:4]), waits=[c1])
            c4 = add("vector", lambda v: v.reciprocal(out=SC1[:, 7:8], in_=SC1[:, 4:5]), waits=[c2])
            c5 = add("vector", lambda v: v.tensor_tensor(
                out=SC1[:, 6:7], in0=SC1[:, 3:4], in1=SC1[:, 4:5], op=ALU.mult), waits=[c1, c2])
            b_sc1.setw(c5)
            bm4 = add("tensor", lambda te: te.matmul(
                out=PSM[:, 0:3], lhsT=ONE[0:1, :], rhs=bass.AP(SC1, 5, [[16, 1], [1, 3]]),
                start=True, stop=True), waits=[c3, c4, c5, c_one] + b_psm.wdeps())
            b_sc1.addr(bm4)
            b_psm.setw(bm4)
            bv4 = add("vector", lambda v: v.tensor_copy(out=SCL[:, 0:3], in_=PSM[:, 0:3]),
                      waits=[bm4] + b_scl.wdeps())
            b_psm.addr(bv4)
            b_scl.setw(bv4)
            bb1 = add("vector", lambda v: v.tensor_scalar(
                out=B1S[:], in0=B1S[:], scalar1=SCL[:, 1:2], scalar2=None, op0=ALU.mult),
                waits=[bv4] + b_b1s.wdeps())
            b_b1s.setw(bb1)
            b_scl.addr(bb1)
            prog.flush()

        w1q_tok = {}
        w1v3 = w1T.rearrange("(c p) h -> p c h", p=128)
        for dc in range(8):
            for qb in range(8):
                sl = (dc * 8 + qb) % 2
                bslot = b_wst[sl]
                dmt = add("sync", lambda sy, dc=dc, qb=qb, sl=sl: sy.dma_start(
                    out=WST[:, sl * 512:(sl + 1) * 512],
                    in_=w1v3[:, dc, qb * 512:(qb + 1) * 512]),
                    waits=bslot.wdeps(), dma=True)
                bslot.setw(dmt)
                rd = add("vector", lambda v, sl=sl: v.tensor_scalar(
                    out=XQI[:, 0:512], in0=WST[:, sl * 512:(sl + 1) * 512],
                    scalar1=SCL[:, 2:3], scalar2=None, op0=ALU.mult),
                    waits=[dmt, bv4] + b_xqi.wdeps())
                bslot.addr(rd)
                b_xqi.setw(rd)
                b_scl.addr(rd)
                extra = (b_ao.wdeps() + b_Q.wdeps()) if (dc == 0 and qb == 0) else []
                cv = add("vector", lambda v, dc=dc, qb=qb: v.tensor_copy(
                    out=pb(R_W1Q + dc * 4096 + qb * 512, 512), in_=XQI[:, 0:512]),
                    waits=[rd] + extra)
                b_xqi.addr(cv)
                w1q_tok[(dc, qb)] = cv
        prog.flush()

        with ExitStack() as ps:
            ph = [ps.enter_context(nc.psum_tensor(f"ph{i}", [128, 512], F32)) for i in range(4)]
            ptq = [ps.enter_context(nc.psum_tensor(f"ptq{i}", [128, 1024], BF16)) for i in range(2)]
            b_ph = [Buf() for _ in range(4)]
            b_ptq = [Buf(), Buf()]
            b_xqf = Buf()

            w2v3 = w2T.rearrange("(c p) d -> p c d", p=128)
            cw = add("vector", lambda v: v.tensor_scalar(
                out=SC1[:, 9:10], in0=SC1[:, 1:2], scalar1=1e-6, scalar2=1.0 / QP,
                op0=ALU.max, op1=ALU.mult), waits=b_sc1.wdeps())
            cr = add("vector", lambda v: v.reciprocal(out=SC1[:, 10:11], in_=SC1[:, 9:10]), waits=[cw])
            b_sc1.setw(cr)
            bm5 = add("tensor", lambda te: te.matmul(
                out=PSM[:, 0:1], lhsT=ONE[0:1, :], rhs=bass.AP(SC1, 10, [[16, 1], [1, 1]]),
                start=True, stop=True), waits=[cr, c_one] + b_psm.wdeps())
            b_sc1.addr(bm5)
            b_psm.setw(bm5)
            bv5a = add("vector", lambda v: v.tensor_copy(out=SCL[:, 3:4], in_=PSM[:, 0:1]),
                       waits=[bm5] + b_scl.wdeps())
            b_psm.addr(bv5a)
            b_scl.setw(bv5a)
            w2q_tok = {}
            w2q_half = {}
            for hc in range(32):
                for half in range(2):
                    sl = (hc * 2 + half) % 2
                    bslot = b_wst[sl]
                    dmt = add("sync", lambda sy, hc=hc, half=half, sl=sl: sy.dma_start(
                        out=WST[:, sl * 512:(sl + 1) * 512],
                        in_=w2v3[:, hc, half * 512:(half + 1) * 512]),
                        waits=bslot.wdeps(), dma=True)
                    bslot.setw(dmt)
                    rd = add("vector", lambda v, sl=sl: v.tensor_scalar(
                        out=HQI[:, 0:512], in0=WST[:, sl * 512:(sl + 1) * 512],
                        scalar1=SCL[:, 3:4], scalar2=None, op0=ALU.mult),
                        waits=[dmt, bv5a] + b_hqi[0].wdeps())
                    bslot.addr(rd)
                    b_hqi[0].setw(rd)
                    b_scl.addr(rd)
                    extra = (b_K.wdeps() + b_V.wdeps()) if (hc == 0 and half == 0) else []
                    cv = add("vector", lambda v, hc=hc, half=half: v.tensor_copy(
                        out=pb(R_W2Q + hc * 1024 + half * 512, 512), in_=HQI[:, 0:512]),
                        waits=[rd] + extra)
                    b_hqi[0].addr(cv)
                    w2q_half[(hc, half)] = cv
                w2q_tok[hc] = cv

            for ss in range(4):
                evs = []
                for q8 in range(8):
                    g = ss * 8 + q8
                    bx = pfbuf(("xst", q8 % 2))
                    xr = add("sync", lambda sy, g=g, sl=q8 % 2: sy.dma_start(
                        out=pf(F_XST + sl * 1024, 1024), in_=x1sp[g * 128:(g + 1) * 128, :]),
                        waits=[spill_x1[g]] + bx.wdeps(), dma=True)
                    bx.setw(xr)
                    rd = add("vector", lambda v, sl=q8 % 2: v.tensor_scalar(
                        out=XQI[:], in0=pf(F_XST + sl * 1024, 1024),
                        scalar1=SCL[:, 0:1], scalar2=None, op0=ALU.mult),
                        waits=[xr, bv4] + b_xqi.wdeps())
                    b_xqi.setw(rd)
                    bx.addr(rd)
                    b_scl.addr(rd)
                    cv = add("vector", lambda v: v.tensor_copy(out=pb(R_XQT, 1024), in_=XQI[:]),
                             waits=[rd])
                    b_xqi.addr(cv)
                    ptqb = b_ptq[q8 % 2]
                    trs = []
                    for dc in range(8):
                        tr = add("tensor", lambda te, dc=dc, pp=ptq[q8 % 2]: te.transpose(
                            out=bass.AP(pp, dc * 128, [[1024, 128], [1, 128]]),
                            in_=pb(R_XQT + dc * 128, 128), identity=pb(R_I, 128)),
                            waits=[cv, c_idn] + (ptqb.wdeps() if dc == 0 else []))
                        trs.append(tr)
                    ptqb.setw(trs[-1])
                    xbuf = q8 // 4
                    xtile = q8 % 4
                    ev = add("vector", lambda v, xbuf=xbuf, xtile=xtile, pp=ptq[q8 % 2]: v.tensor_copy(
                        out=bass.AP(PB, R_W0 + xbuf * 4096 + xtile * 128, [[84480, 128], [512, 8], [1, 128]]),
                        in_=pp[:]),
                        waits=trs + (b_xqf.wdeps() if q8 == 0 else []))
                    ptqb.addr(ev)
                    evs.append(ev)
                b_xqf.setw(evs[-1])
                if "ffn" in taps and ss == 0:
                    t1 = add("sync", lambda sy: sy.dma_start(
                        out=tap_aps["tap_xq"], in_=pb(R_W0, 8192)), waits=[evs[-1]], sem="ST", dma=True)
                    b_xqf.addr(t1)
                    t2 = add("sync", lambda sy: sy.dma_start(
                        out=tap_aps["tap_w1q"], in_=pb(R_W1Q, 32768)),
                        waits=[w1q_tok[(7, 7)]], sem="ST", dma=True)
                    t3 = add("sync", lambda sy: sy.dma_start(
                        out=tap_aps["tap_scl"], in_=SCL[:]), waits=[bv5a], sem="ST", dma=True)
                    b_scl.addr(t3)
                for oc in range(32):
                    bank0 = (oc % 2) * 2
                    for dc in range(8):
                        for half in range(2):
                            pjb = b_ph[bank0 + half]
                            deps = [evs[-1], w1q_tok[(dc, oc // 4)]]
                            if dc == 0:
                                deps = deps + pjb.wdeps()
                            mtok = add("tensor", lambda te, oc=oc, dc=dc, half=half,
                                       pp=ph[bank0 + half]: te.matmul(
                                out=pp[:],
                                lhsT=pb(R_W1Q + dc * 4096 + oc * 128, 128),
                                rhs=pb(R_W0 + half * 4096 + dc * 512, 512),
                                start=(dc == 0), stop=(dc == 7)),
                                waits=deps)
                            pjb.setw(mtok)
                            b_xqf.addr(mtok)
                    for half in range(2):
                        pjb = b_ph[bank0 + half]
                        ts = ss * 2 + half
                        bh = pfbuf(("hst", half))
                        gl = add("scalar", lambda sce, oc=oc, half=half, pp=ph[bank0 + half]: sce.activation(
                            out=pf(F_HST + half * 512, 512), in_=pp[:], func=AF.Gelu,
                            bias=B1S[:, oc:oc + 1], scale=SCL[:, 1:2]),
                            waits=[pjb.wtok, bb1, bv4] + bh.wdeps())
                        pjb.addr(gl)
                        bh.setw(gl)
                        b_b1s.addr(gl); b_scl.addr(gl)
                        m1 = add("vector", lambda v, half=half: v.tensor_reduce(
                            out=ST8[:, 33:34], in_=pf(F_HST + half * 512, 512), axis=AX.X, op=ALU.max,
                            apply_absolute_value=True),
                            waits=[gl] + b_sth.wdeps())
                        b_sth.setw(m1)
                        m2 = add("vector", lambda v: v.tensor_tensor(
                            out=MX[:, 3:4], in0=MX[:, 3:4], in1=ST8[:, 33:34], op=ALU.max),
                            waits=[m1] + b_mx[3].wdeps())
                        b_mx[3].setw(m2)
                        b_sth.addr(m2)
                        bh.addr(m1)
                        sp = add("sync", lambda sy, ts=ts, oc=oc, half=half: sy.dma_start(
                            out=hsp[ts, oc], in_=pf(F_HST + half * 512, 512)),
                            waits=[gl], sem="ST", dma=True)
                        bh.addr(sp)
                        spill_h[(ts, oc)] = sp
                        if "h" in taps:
                            sp2 = add("sync", lambda sy, ts=ts, oc=oc, half=half: sy.dma_start(
                                out=tap_aps["tap_h"][ts, oc], in_=pf(F_HST + half * 512, 512)),
                                waits=[gl], sem="ST", dma=True)
                            bh.addr(sp2)
            prog.flush()

        with ExitStack() as ps:
            pou = [ps.enter_context(nc.psum_tensor(f"pou{i}", [128, 512], F32)) for i in range(4)]
            b_pou = [Buf() for _ in range(4)]

            tt5 = add("tensor", lambda te: te.transpose(out=PTR[:], in_=MX[:, 3:4], identity=IFP[:]),
                      waits=b_mx[3].rdeps() + [c_if] + b_ptrg.wdeps())
            b_mx[3].addr(tt5)
            b_ptrg.setw(tt5)
            t25 = add("vector", lambda v: v.tensor_reduce(
                out=SC1[:, 11:12], in_=PTR[0:1, :], axis=AX.X, op=ALU.max),
                waits=[tt5] + b_sc1.wdeps())
            b_ptrg.addr(t25)
            b_sc1.setw(t25)
            zz5 = add("gpsimd", lambda g: g.memset(SC1[:, 8:9], 0.0), waits=[t25])
            cp5 = add("vector", lambda v: v.tensor_copy(out=SC1[:, 8:9], in_=SC1[:, 11:12]), waits=[zz5])
            zx5 = add("gpsimd", lambda g: g.memset(SC1[:, 12:16], 0.0), waits=[cp5])
            b_sc1.setw(zx5)
            d15 = add("sync", lambda sy: sy.dma_start(out=cci2, in_=SC1[:, 8:16]), waits=[cp5, zx5],
                      sem="ST", dma=True)
            cc2 = add("gpsimd", lambda g: g.collective_compute(
                "AllReduce", ALU.max, replica_groups=[list(range(NCORES))],
                ins=[cci2], outs=[cco2]), waits=[d15], sem="C", step=1)
            d25 = add("sync", lambda sy: sy.dma_start(out=SC1[:, 8:16], in_=cco2), waits=[cc2], dma=True)
            b_sc1.setw(d25)
            c15 = add("vector", lambda v: v.tensor_scalar(
                out=SC1[:, 11:12], in0=SC1[:, 8:9], scalar1=1e-6, scalar2=1.0 / QP,
                op0=ALU.max, op1=ALU.mult), waits=[d25])
            c35 = add("vector", lambda v: v.reciprocal(out=SC1[:, 12:13], in_=SC1[:, 11:12]), waits=[c15])
            c55 = add("vector", lambda v: v.tensor_tensor(
                out=SC1[:, 13:14], in0=SC1[:, 11:12], in1=SC1[:, 9:10], op=ALU.mult), waits=[c15])
            b_sc1.setw(c55)
            bm5b = add("tensor", lambda te: te.matmul(
                out=PSM[:, 0:2], lhsT=ONE[0:1, :], rhs=bass.AP(SC1, 12, [[16, 1], [1, 2]]),
                start=True, stop=True), waits=[c35, c55, c_one] + b_psm.wdeps())
            b_sc1.addr(bm5b)
            b_psm.setw(bm5b)
            bv5 = add("vector", lambda v: v.tensor_copy(out=SCL[:, 4:6], in_=PSM[:, 0:2]),
                      waits=[bm5b] + b_scl.wdeps())
            b_psm.addr(bv5)
            b_scl.setw(bv5)
            bsc = add("vector", lambda v: v.tensor_scalar(
                out=B2R[:], in0=B2R[:], scalar1=bass.AP(SC1, 13, [[16, 1], [1, 1]]), scalar2=None,
                op0=ALU.mult), waits=[c55] + b_b2r.wdeps())
            b_b2r.setw(bsc)
            b_sc1.addr(bsc)
            b2toks = []
            for dh in range(2):
                bmb = add("tensor", lambda te, dh=dh, pp=pou[dh]: te.matmul(
                    out=pp[:], lhsT=ONE[0:1, :], rhs=B2R[0:1, dh * 512:(dh + 1) * 512],
                    start=True, stop=True), waits=[bsc, c_one] + b_pou[dh].wdeps())
                b_pou[dh].setw(bmb)
                b_b2r.addr(bmb)
                bwv = add("vector", lambda v, dh=dh, pp=pou[dh]: v.tensor_copy(
                    out=WST[:, dh * 512:(dh + 1) * 512], in_=pp[:]),
                    waits=[bmb] + b_wst[0].wdeps() + b_wst[1].wdeps())
                b_pou[dh].addr(bwv)
                b2toks.append(bwv)
            b_wst[0].setw(b2toks[0])
            b_wst[1].setw(b2toks[1])

            b_hqf = [Buf() for _ in range(32)]
            for ts in range(8):
                slab_tok = {}
                for hc in range(32):
                    sl = hc % 2
                    bsl = pfbuf(("hsl", sl))
                    dmt = add("sync", lambda sy, ts=ts, hc=hc, sl=sl: sy.dma_start(
                        out=pf(F_HSL + sl * 512, 512), in_=hsp[ts, hc]),
                        waits=[spill_h[(ts, hc)]] + bsl.wdeps(), dma=True)
                    bsl.setw(dmt)
                    rdi = add("scalar", lambda sce, sl=sl, hq=hc % 2: sce.activation(
                        out=HQI[:, hq * 512:hq * 512 + 512], in_=pf(F_HSL + sl * 512, 512),
                        func=AF.Identity, scale=SCL[:, 4:5]),
                        waits=[dmt, bv5] + b_hqi[hc % 2].wdeps())
                    bsl.addr(rdi)
                    b_hqi[hc % 2].setw(rdi)
                    b_scl.addr(rdi)
                    cv = add("vector", lambda v, hc=hc, hq=hc % 2: v.tensor_copy(
                        out=pb(R_HQ + hc * 512, 512), in_=HQI[:, hq * 512:hq * 512 + 512]),
                        waits=[rdi] + b_hqf[hc].wdeps())
                    b_hqi[hc % 2].addr(cv)
                    b_hqf[hc].setw(cv)
                    slab_tok[hc] = cv
                for q4 in range(4):
                    bank0 = (q4 % 2) * 2
                    for hc in range(32):
                        for dh in range(2):
                            pjb = b_pou[bank0 + dh]
                            deps = [slab_tok[hc], w2q_tok[hc]]
                            if hc == 0:
                                deps = deps + pjb.wdeps()
                            mtok = add("tensor", lambda te, q4=q4, dh=dh, hc=hc,
                                       pp=pou[bank0 + dh]: te.matmul(
                                out=pp[:],
                                lhsT=pb(R_HQ + hc * 512 + q4 * 128, 128),
                                rhs=pb(R_W2Q + hc * 1024 + dh * 512, 512),
                                start=(hc == 0), stop=(hc == 31)),
                                waits=deps)
                            pjb.setw(mtok)
                            b_hqf[hc].addr(mtok)
                    g = ts * 4 + q4
                    bx1 = pfbuf(("x1b", q4 % 2))
                    xr = add("sync", lambda sy, g=g, sl=q4 % 2: sy.dma_start(
                        out=pf(F_X1B + sl * 1024, 1024), in_=x1sp[g * 128:(g + 1) * 128, :]),
                        waits=[spill_x1[g]] + bx1.wdeps(), dma=True)
                    bx1.setw(xr)
                    bo = pfbuf(("ost", q4 % 2))
                    rts = []
                    for dh in range(2):
                        pjb = b_pou[bank0 + dh]
                        r1 = add("vector", lambda v, dh=dh, sl=q4 % 2, pp=pou[bank0 + dh]: v.scalar_tensor_tensor(
                            out=pf(F_OST + sl * 1024 + dh * 512, 512), in0=pp[:],
                            scalar=SCL[:, 5:6], in1=WST[:, dh * 512:(dh + 1) * 512],
                            op0=ALU.mult, op1=ALU.add),
                            waits=[pjb.wtok, bv5] + b2toks + (bo.wdeps() if dh == 0 else []))
                        pjb.addr(r1)
                        b_wst[dh].addr(r1)
                        b_scl.addr(r1)
                        rts.append(r1)
                        if dh == 0:
                            bo.setw(r1)
                        else:
                            bo.wtok = r1
                    r2 = add("vector", lambda v, sl=q4 % 2: v.tensor_tensor(
                        out=pf(F_OST + sl * 1024, 1024), in0=pf(F_OST + sl * 1024, 1024),
                        in1=pf(F_X1B + sl * 1024, 1024), op=ALU.add),
                        waits=[xr] + rts)
                    bo.wtok = r2
                    bx1.addr(r2)
                    fo = ln_chain(q4 % 2, F_OST + (q4 % 2) * 1024, F_X1B + (q4 % 2) * 1024,
                                  [r2], bx1.wdeps())
                    bx1.setw(fo)
                    bo.addr(fo)
                    so = add("sync", lambda sy, g=g, sl=q4 % 2: sy.dma_start(
                        out=out[g // 16, (g % 16) * 128:((g % 16) + 1) * 128, :],
                        in_=pf(F_X1B + sl * 1024, 1024)),
                        waits=[fo], sem="ST", dma=True)
                    bx1.addr(so)
            prog.flush()

        prog.final_wait("sync", "ST")

    return nc


_NC_CACHE = {}


def _get_nc(taps=()):
    key = tuple(sorted(taps))
    if key not in _NC_CACHE:
        _NC_CACHE[key] = build_nc(taps=key)
    return _NC_CACHE[key]


def make_in_maps(inputs):
    x = np.asarray(inputs["L0_attn_x"], dtype=np.float32)
    wq = np.asarray(inputs["L0_attn_q_proj_weight"], dtype=np.float32)
    wk = np.asarray(inputs["L0_attn_k_proj_weight"], dtype=np.float32)
    wv = np.asarray(inputs["L0_attn_v_proj_weight"], dtype=np.float32)
    wo = np.asarray(inputs["L0_attn_o_proj_weight"], dtype=np.float32)
    w1 = np.asarray(inputs["L0_ffn_w1"], dtype=np.float32)
    b1 = np.asarray(inputs["L0_ffn_b1"], dtype=np.float32)
    w2 = np.asarray(inputs["L0_ffn_w2"], dtype=np.float32)
    b2 = np.asarray(inputs["L0_ffn_b2"], dtype=np.float32)
    shared = {
        "wqT": np.ascontiguousarray(wq.T),
        "wkT": np.ascontiguousarray(wk.T),
        "wvT": np.ascontiguousarray(wv.T),
        "woT": np.ascontiguousarray(wo.T),
        "w1T": np.ascontiguousarray(w1.T),
        "w2T": np.ascontiguousarray(w2.T),
        "b1c": np.ascontiguousarray(b1.reshape(32, 128).T),
        "b2r": np.ascontiguousarray(b2.reshape(1, D)),
    }
    in_maps = []
    for c in range(NCORES):
        xs = x[NSEQ * c:NSEQ * (c + 1)]
        m = dict(shared)
        m["x_tm"] = np.ascontiguousarray(xs)
        m["x_fm"] = np.ascontiguousarray(xs.transpose(0, 2, 1))
        in_maps.append(m)
    return in_maps


def kernel(**inputs):
    nc = _get_nc()
    in_maps = make_in_maps(inputs)
    res = run_bass_kernel_spmd(nc, in_maps, core_ids=list(range(NCORES)))
    outs = [res.results[c]["out"].reshape(NSEQ, S, D) for c in range(NCORES)]
    return np.concatenate(outs, axis=0).astype(np.float32)
